# version 01
# baseline (speedup 1.0000x reference)
import numpy as np
import ml_dtypes
from contextlib import ExitStack

import concourse.bass as bass
import concourse.mybir as mybir
from concourse.bass_utils import run_bass_kernel_spmd

F32 = mybir.dt.float32
BF16 = mybir.dt.bfloat16
U32 = mybir.dt.uint32
AF = mybir.ActivationFunctionType
OP = mybir.AluOpType
AX = mybir.AxisListType.X

S = 4096
D = 1024
H = 16
NP = 8
DK = 64
E = 8
TB = 1024
NTB = S // TB
NT = S // 128
NQB = S // 512
EPS = 1e-5

SUBLAYERS = ("attn_a", "ffn", "attn_b", "ffn")


class Ctr:
    def __init__(self, nc, es, name):
        self.h = es.enter_context(nc.semaphore(name))
        self.n = 0

    def inc(self, inst, k=1):
        inst.then_inc(self.h, k)
        self.n += k
        return self.n

    def dma(self, inst):
        return self.inc(inst, 16)


def build_nc(n_sub=4):
    nc = bass.Bass("TRN2", target_bir_lowering=False)

    x0 = nc.dram_tensor("x0", [S, D], F32, kind="ExternalInput").ap()
    out = nc.dram_tensor("out", [S, D], F32, kind="ExternalOutput").ap()
    wqkv_a = nc.dram_tensor("wqkv_a", [NP, 128, 3072], F32, kind="ExternalInput").ap()
    wqkv_b = nc.dram_tensor("wqkv_b", [NP, 128, 3072], F32, kind="ExternalInput").ap()
    wo_a = nc.dram_tensor("wo_a", [128, 8192], F32, kind="ExternalInput").ap()
    wo_b = nc.dram_tensor("wo_b", [128, 8192], F32, kind="ExternalInput").ap()
    bqk_a = nc.dram_tensor("bqk_a", [128, 16], F32, kind="ExternalInput").ap()
    bv_a = nc.dram_tensor("bv_a", [1, 1024], F32, kind="ExternalInput").ap()
    bo_a = nc.dram_tensor("bo_a", [1, 1024], F32, kind="ExternalInput").ap()
    cmask = nc.dram_tensor("cmask", [128, 2048], F32, kind="ExternalInput").ap()
    idn_h = nc.dram_tensor("idn_h", [128, 128], F32, kind="ExternalInput").ap()
    w1h = nc.dram_tensor("w1h", [E, 32, 128, 2048], F32, kind="ExternalInput").ap()
    w2h = nc.dram_tensor("w2h", [E, 2, 32, 128, 512], F32, kind="ExternalInput").ap()
    b1h = nc.dram_tensor("b1h", [128, E * 64], F32, kind="ExternalInput").ap()
    b2h = nc.dram_tensor("b2h", [1, E * 1024], F32, kind="ExternalInput").ap()
    rwh = nc.dram_tensor("rwh", [128, 64], F32, kind="ExternalInput").ap()
    rbh = nc.dram_tensor("rbh", [1, 8], F32, kind="ExternalInput").ap()
    xresA = nc.dram_tensor("xresA", [S, D], F32, kind="Internal").ap()
    xresB = nc.dram_tensor("xresB", [S, D], F32, kind="Internal").ap()
    xln_d = nc.dram_tensor("xln_d", [S, D], BF16, kind="Internal").ap()
    idx_d = nc.dram_tensor("idx_d", [1, S], BF16, kind="Internal").ap()

    res_chain = [x0, xresA, xresB, xresA]
    res_dst = [xresA, xresB, xresA, None]

    with ExitStack() as es:
        EC = es.enter_context
        P = [EC(nc.psum_tensor(f"P{i}", [128, 512], F32)) for i in range(8)]
        xr = [EC(nc.sbuf_tensor(f"xr{i}", [128, D], F32)) for i in range(2)]
        xnw = [EC(nc.sbuf_tensor(f"xnw{i}", [128, D], F32)) for i in range(2)]
        xlb = [EC(nc.sbuf_tensor(f"xlb{i}", [128, D], BF16)) for i in range(2)]
        sqd = EC(nc.sbuf_tensor("sqd", [128, D], F32))
        stt = [EC(nc.sbuf_tensor(f"stt{i}", [128, 8], F32)) for i in range(2)]
        ones = EC(nc.sbuf_tensor("ones", [1, 512], BF16))
        cm = EC(nc.sbuf_tensor("cm", [128, 2048], BF16))
        bqk = EC(nc.sbuf_tensor("bqk", [128, 16], F32))
        bvr = EC(nc.sbuf_tensor("bvr", [1, 1024], BF16))
        bor = EC(nc.sbuf_tensor("bor", [1, 1024], BF16))
        b2r = EC(nc.sbuf_tensor("b2r", [1, E * 1024], BF16))
        b1s = EC(nc.sbuf_tensor("b1s", [128, E * 64], F32))
        rws = EC(nc.sbuf_tensor("rws", [128, 64], BF16))
        rbs = EC(nc.sbuf_tensor("rbs", [1, 8], BF16))
        idn = EC(nc.sbuf_tensor("idn", [128, 128], BF16))
        r8 = EC(nc.sbuf_tensor("r8", [128, 9], F32))
        r8b = EC(nc.sbuf_tensor("r8b", [128, 8], F32))
        idx8 = EC(nc.sbuf_tensor("idx8", [128, 8], U32))
        idxb = EC(nc.sbuf_tensor("idxb", [128, 1], BF16))
        idxT = EC(nc.sbuf_tensor("idxT", [1, TB], BF16))
        mrow = [EC(nc.sbuf_tensor(f"mrow{i}", [1, TB], BF16)) for i in range(2)]
        recs = EC(nc.sbuf_tensor("recs", [1, 512], BF16))
        rbb = EC(nc.sbuf_tensor("rbb", [128, 512], BF16))
        c1s = EC(nc.sbuf_tensor("c1s", [128, 512], BF16))
        epsc = EC(nc.sbuf_tensor("epsc", [128, 1], F32))

        CT = Ctr(nc, es, "CT")
        CV = Ctr(nc, es, "CV")
        CS = Ctr(nc, es, "CS")
        CG = Ctr(nc, es, "CG")
        CX = Ctr(nc, es, "CX")
        CD = Ctr(nc, es, "CD")

        blk = EC(nc.Block())

        def waits(eng, conds):
            for c, v in conds:
                if c is not None and v > 0:
                    eng.wait_ge(c.h, v)

        pfree = [[] for _ in range(8)]

        @blk.gpsimd
        def _(g):
            CG.dma(g.dma_start(out=cm[:, :], in_=cmask))
            CG.dma(g.dma_start(out=bqk[:, :], in_=bqk_a))
            CG.dma(g.dma_start(out=bvr[:, :], in_=bv_a))
            CG.dma(g.dma_start(out=bor[:, :], in_=bo_a))
            CG.dma(g.dma_start(out=b2r[:, :], in_=b2h))
            CG.dma(g.dma_start(out=b1s[:, :], in_=b1h))
            CG.dma(g.dma_start(out=rws[:, :], in_=rwh))
            CG.dma(g.dma_start(out=rbs[:, :], in_=rbh))
            CG.dma(g.dma_start(out=idn[:, :], in_=idn_h))
        g_consts = CG.n

        @blk.vector
        def _(v):
            v.memset(epsc[:, :], EPS)
            CV.inc(v.memset(ones[:, :], 1.0))
        v_ones = CV.n

        bst = {"xr": [[], []], "xnw": [[], []], "xlb": [[], []], "st": [[], []]}

        def ln_chain(par, src_done_list, tag_last):
            st = stt[par]

            @blk.vector
            def _(v):
                waits(v, src_done_list)
                waits(v, bst["st"][par])
                CV.inc(v.reduce_sum(st[:, 0:1], xnw[par][:, :], axis=AX))
            sum_done = CV.n

            @blk.scalar
            def _(s):
                waits(s, src_done_list)
                waits(s, bst["st"][par])
                CS.inc(s.activation(sqd[:, :], xnw[par][:, :], AF.Square,
                                    accum_out=st[:, 1:2]))
            sq_done = CS.n

            @blk.vector
            def _(v):
                v.wait_ge(CV.h, sum_done)
                v.wait_ge(CS.h, sq_done)
                CV.inc(v.tensor_scalar_mul(st[:, 2:3], st[:, 0:1], -1.0 / D))
                v.wait_ge(CV.h, CV.n)
                v.tensor_tensor(st[:, 3:4], st[:, 2:3], st[:, 2:3], OP.mult)
                CV.inc(v.tensor_scalar_mul(st[:, 4:5], st[:, 1:2], 1.0 / D))
                v.wait_ge(CV.h, CV.n)
                CV.inc(v.tensor_tensor(st[:, 5:6], st[:, 4:5], st[:, 3:4], OP.subtract))
            var_done = CV.n

            @blk.scalar
            def _(s):
                s.wait_ge(CV.h, var_done)
                CS.inc(s.activation(st[:, 6:7], st[:, 5:6], AF.Sqrt, bias=epsc[:, 0:1]))
            sd_done = CS.n

            @blk.vector
            def _(v):
                v.wait_ge(CS.h, sd_done)
                waits(v, bst["xlb"][par])
                CV.inc(v.reciprocal(st[:, 7:8], st[:, 6:7]))
                v.wait_ge(CV.h, CV.n)
                CV.inc(v.tensor_scalar(xlb[par][:, :], xnw[par][:, :], st[:, 2:3],
                                       st[:, 7:8], OP.add, OP.mult))
            xlb_done = CV.n
            bst["st"][par] = [(CV, xlb_done), (CS, sd_done)]
            return xlb_done, sq_done

        def boundary_tile(k, tt, src_mode, src_info, is_last):
            par = tt % 2
            rsrc = res_chain[k]
            rdst = out if is_last else res_dst[k]

            @blk.sync
            def _(sy):
                waits(sy, bst["xr"][par])
                CX.dma(sy.dma_start(out=xr[par][:, :], in_=rsrc[tt * 128:(tt + 1) * 128, :]))
            x_loaded = CX.n

            @blk.vector
            def _(v):
                v.wait_ge(CX.h, x_loaded)
                waits(v, bst["xnw"][par])
                if src_mode == "psum":
                    blo, bhi, ctlo, cthi = src_info
                    v.wait_ge(CT.h, ctlo)
                    v.tensor_tensor(xnw[par][:, 0:512], P[blo][:, :], xr[par][:, 0:512], OP.add)
                    v.wait_ge(CT.h, cthi)
                    CV.inc(v.tensor_tensor(xnw[par][:, 512:1024], P[bhi][:, :],
                                           xr[par][:, 512:1024], OP.add))
                else:
                    acc, coff, cvr = src_info
                    v.wait_ge(CV.h, cvr)
                    CV.inc(v.tensor_tensor(xnw[par][:, :], acc[:, coff:coff + D],
                                           xr[par][:, :], OP.add))
            xnew_done = CV.n
            bst["xr"][par] = [(CV, xnew_done)]
            if src_mode == "psum":
                pfree[src_info[0]] = [(CV, xnew_done)]
                pfree[src_info[1]] = [(CV, xnew_done)]

            @blk.sync
            def _(sy):
                sy.wait_ge(CV.h, xnew_done)
                CX.dma(sy.dma_start(out=rdst[tt * 128:(tt + 1) * 128, :], in_=xnw[par][:, :]))
            xnew_stored = CX.n

            if is_last:
                bst["xnw"][par] = [(CX, xnew_stored)]
                return xnew_done

            xlb_done, sq_done = ln_chain(par, [], is_last)
            bst["xnw"][par] = [(CX, xnew_stored), (CV, xlb_done), (CS, sq_done)]

            @blk.sync
            def _(sy):
                sy.wait_ge(CV.h, xlb_done)
                CX.dma(sy.dma_start(out=xln_d[tt * 128:(tt + 1) * 128, :], in_=xlb[par][:, :]))
            bst["xlb"][par] = [(CX, CX.n)]
            return xnew_done

        for tt in range(NT):
            par = tt % 2

            @blk.sync
            def _(sy):
                waits(sy, bst["xnw"][par])
                CX.dma(sy.dma_start(out=xnw[par][:, :], in_=x0[tt * 128:(tt + 1) * 128, :]))
            x_loaded = CX.n

            xlb_done, sq_done = ln_chain(par, [(CX, x_loaded)], False)
            bst["xnw"][par] = [(CV, xlb_done), (CS, sq_done)]

            @blk.sync
            def _(sy):
                sy.wait_ge(CV.h, xlb_done)
                CX.dma(sy.dma_start(out=xln_d[tt * 128:(tt + 1) * 128, :], in_=xlb[par][:, :]))
            bst["xlb"][par] = [(CX, CX.n)]

        xln_ready = CX.n

        def emit_attn(k, kind, is_last):
            nonlocal xln_ready
            causal = (kind == "a")
            wqkv = wqkv_a if causal else wqkv_b
            wo = wo_a if causal else wo_b
            scale = 0.125 if causal else 1.0 / 65536.0
            sub_start_cx = xln_ready

            def off_w(s_, c, hh):
                return ((s_ * 8 + c) * 2 + hh) * 64

            with ExitStack() as aes:
                A = aes.enter_context
                xs = [A(nc.sbuf_tensor(f"s{k}_xs{i}", [128, 4096], BF16)) for i in range(2)]
                qt = A(nc.sbuf_tensor(f"s{k}_qt", [128, S], BF16))
                ktb = A(nc.sbuf_tensor(f"s{k}_ktb", [128, S], BF16))
                vsb = [A(nc.sbuf_tensor(f"s{k}_vsb{i}", [128, NT * 65], BF16)) for i in range(2)]
                pt = [A(nc.sbuf_tensor(f"s{k}_pt{i}", [128, 512], BF16)) for i in range(4)]
                wq = [A(nc.sbuf_tensor(f"s{k}_wq{i}", [128, 3072], BF16)) for i in range(2)]
                wos = A(nc.sbuf_tensor(f"s{k}_wos", [128, 8192], BF16))
                cc = A(nc.sbuf_tensor(f"s{k}_cc", [128, 8 * S], BF16))

                @blk.gpsimd
                def _(g):
                    g.wait_ge(CX.h, sub_start_cx)
                    CG.dma(g.dma_start(out=wos[:, :], in_=wo))
                g_wo = CG.n

                @blk.vector
                def _(v):
                    for i in range(2):
                        va = bass.AP(vsb[i], 64, [[NT * 65, 128], [65, NT]])
                        inst = v.memset(va, 1.0)
                    CV.inc(inst)
                v_vones = CV.n

                xs_free = [[], []]
                wq_free = [[], []]
                pt_free = [[]] * 4
                cc_all = 0

                for p in range(NP):
                    wpar = p % 2

                    @blk.gpsimd
                    def _(g):
                        waits(g, wq_free[wpar])
                        CG.dma(g.dma_start(out=wq[wpar][:, :], in_=wqkv[p]))
                    g_w = CG.n

                    last_vps = 0
                    for qb in range(NQB):
                        xpar = qb % 2

                        @blk.scalar
                        def _(sq):
                            waits(sq, xs_free[xpar])
                            sq.wait_ge(CX.h, xln_ready)
                            for c in range(8):
                                CD.dma(sq.dma_start(
                                    out=xs[xpar][:, c * 512:(c + 1) * 512],
                                    in_=xln_d[qb * 512:(qb + 1) * 512, c * 128:(c + 1) * 128],
                                    transpose=True))
                        x_ld = CD.n

                        @blk.tensor
                        def _(t):
                            t.wait_ge(CG.h, g_w)
                            t.wait_ge(CD.h, x_ld)
                            waits(t, pfree[0])
                            for hh in range(2):
                                for c in range(8):
                                    woff = off_w(0, c, hh)
                                    tp = None if hh == 0 else (0, 64)
                                    inst = t.matmul(P[0][hh * 64:(hh + 1) * 64, :],
                                                    wq[wpar][:, woff:woff + 64],
                                                    xs[xpar][:, c * 512:(c + 1) * 512],
                                                    start=(c == 0), stop=(c == 7),
                                                    tile_position=tp)
                            CT.inc(inst)
                            waits(t, pfree[1])
                            for hh in range(2):
                                for c in range(8):
                                    woff = off_w(1, c, hh)
                                    tp = None if hh == 0 else (0, 64)
                                    inst = t.matmul(P[1][hh * 64:(hh + 1) * 64, :],
                                                    wq[wpar][:, woff:woff + 64],
                                                    xs[xpar][:, c * 512:(c + 1) * 512],
                                                    start=(c == 0), stop=(c == 7),
                                                    tile_position=tp)
                            CT.inc(inst)
                            waits(t, pfree[2])
                            t.wait_ge(CV.h, v_vones)
                            for sub in range(4):
                                for c in range(8):
                                    inst = t.matmul(
                                        P[2][:, sub * 128:(sub + 1) * 128],
                                        xs[xpar][:, c * 512 + sub * 128:c * 512 + (sub + 1) * 128],
                                        wq[wpar][:, off_w(2, c, 0):off_w(2, c, 0) + 128],
                                        start=(c == 0), stop=(c == 7 and not causal))
                                if causal:
                                    inst = t.matmul(P[2][:, sub * 128:(sub + 1) * 128],
                                                    ones[0:1, 0:128],
                                                    bvr[0:1, p * 128:(p + 1) * 128],
                                                    start=False, stop=True)
                            CT.inc(inst)
                        qt_ps = CT.n - 2
                        kt_ps = CT.n - 1
                        v_ps = CT.n
                        last_vps = v_ps

                        @blk.vector
                        def _(v):
                            v.wait_ge(CT.h, qt_ps)
                            if causal:
                                CV.inc(v.tensor_scalar_add(qt[:, qb * 512:(qb + 1) * 512],
                                                           P[0][:, :], bqk[:, p:p + 1]))
                            else:
                                CV.inc(v.tensor_copy(qt[:, qb * 512:(qb + 1) * 512], P[0][:, :]))
                            v.wait_ge(CT.h, kt_ps)
                            if causal:
                                CV.inc(v.tensor_scalar_add(ktb[:, qb * 512:(qb + 1) * 512],
                                                           P[1][:, :], bqk[:, 8 + p:9 + p]))
                            else:
                                CV.inc(v.tensor_copy(ktb[:, qb * 512:(qb + 1) * 512], P[1][:, :]))
                            v.wait_ge(CT.h, v_ps)
                            for sub in range(4):
                                kti = qb * 4 + sub
                                v.tensor_copy(vsb[0][:, kti * 65:kti * 65 + 64],
                                              P[2][:, sub * 128:sub * 128 + 64])
                                inst = v.tensor_copy(vsb[1][:, kti * 65:kti * 65 + 64],
                                                     P[2][:, sub * 128 + 64:sub * 128 + 128])
                            CV.inc(inst)
                        pfree[0] = [(CV, CV.n - 2)]
                        pfree[1] = [(CV, CV.n - 1)]
                        pfree[2] = [(CV, CV.n)]
                        xs_free[xpar] = [(CT, v_ps)]
                    qkv_evacs = CV.n
                    wq_free[wpar] = [(CT, last_vps)]

                    for qb in range(NQB):
                        nkt = 4 * (qb + 1) if causal else NT
                        for kti in range(nkt):
                            ppar = kti % 2

                            @blk.tensor
                            def _(t):
                                if kti == 0:
                                    t.wait_ge(CV.h, qkv_evacs)
                                waits(t, pfree[2])
                                CT.inc(t.matmul(P[2][:, :], ktb[0:64, kti * 128:(kti + 1) * 128],
                                                qt[0:64, qb * 512:(qb + 1) * 512],
                                                start=True, stop=True))
                                waits(t, pfree[3])
                                CT.inc(t.matmul(P[3][:, :], ktb[64:128, kti * 128:(kti + 1) * 128],
                                                qt[64:128, qb * 512:(qb + 1) * 512],
                                                start=True, stop=True, tile_position=(64, 0)))
                            s0_ps = CT.n - 1
                            s1_ps = CT.n

                            r = kti - 4 * qb
                            masked = causal and 0 <= r <= 3

                            @blk.scalar
                            def _(sq):
                                sq.wait_ge(CT.h, s0_ps)
                                waits(sq, pt_free[ppar])
                                CS.inc(sq.activation(pt[ppar][:, :], P[2][:, :], AF.Exp,
                                                     scale=scale))
                                sq.wait_ge(CT.h, s1_ps)
                                waits(sq, pt_free[2 + ppar])
                                CS.inc(sq.activation(pt[2 + ppar][:, :], P[3][:, :], AF.Exp,
                                                     scale=scale))
                            exp0 = CS.n - 1
                            exp1 = CS.n
                            pfree[2] = [(CS, exp0)]
                            pfree[3] = [(CS, exp1)]

                            if masked:
                                @blk.vector
                                def _(v):
                                    v.wait_ge(CS.h, exp1)
                                    v.tensor_tensor(pt[ppar][:, :], pt[ppar][:, :],
                                                    cm[:, r * 512:(r + 1) * 512], OP.mult)
                                    CV.inc(v.tensor_tensor(pt[2 + ppar][:, :], pt[2 + ppar][:, :],
                                                           cm[:, r * 512:(r + 1) * 512], OP.mult))
                                ready = [(CV, CV.n)]
                            else:
                                ready = [(CS, exp1)]

                            @blk.tensor
                            def _(t):
                                waits(t, ready)
                                if kti == 0:
                                    waits(t, pfree[4])
                                    waits(t, pfree[5])
                                t.matmul(P[4][0:65, :], vsb[0][:, kti * 65:(kti + 1) * 65],
                                         pt[ppar][:, :], start=(kti == 0), stop=(kti == nkt - 1))
                                CT.inc(t.matmul(P[5][0:65, :], vsb[1][:, kti * 65:(kti + 1) * 65],
                                                pt[2 + ppar][:, :], start=(kti == 0),
                                                stop=(kti == nkt - 1)))
                            pv_done = CT.n
                            pt_free[ppar] = [(CT, pv_done)]
                            pt_free[2 + ppar] = [(CT, pv_done)]

                        ccl = p * S + qb * 512

                        @blk.vector
                        def _(v):
                            v.wait_ge(CT.h, pv_done)
                            with nc.allow_low_precision(reason="bf16 matmul operand; fp32 psum accumulation"):
                                CV.inc(v.reciprocal(recs[0:1, :], P[4][64:65, :]))
                        rec0 = CV.n

                        @blk.tensor
                        def _(t):
                            t.wait_ge(CV.h, rec0)
                            waits(t, pfree[0])
                            CT.inc(t.matmul(P[0][0:64, :], ones[0:1, 0:64], recs[0:1, :],
                                            start=True, stop=True))
                        rb0_ps = CT.n

                        @blk.vector
                        def _(v):
                            v.wait_ge(CT.h, rb0_ps)
                            CV.inc(v.tensor_copy(rbb[0:64, :], P[0][0:64, :]))
                            v.wait_ge(CV.h, CV.n)
                            CV.inc(v.tensor_tensor(cc[0:64, ccl:ccl + 512], P[4][0:64, :],
                                                   rbb[0:64, :], OP.mult))
                            with nc.allow_low_precision(reason="bf16 matmul operand; fp32 psum accumulation"):
                                CV.inc(v.reciprocal(recs[0:1, :], P[5][64:65, :]))
                        cc0_done = CV.n - 1
                        rec1 = CV.n
                        pfree[4] = [(CV, cc0_done)]

                        @blk.tensor
                        def _(t):
                            t.wait_ge(CV.h, rec1)
                            CT.inc(t.matmul(P[0][0:64, :], ones[0:1, 0:64], recs[0:1, :],
                                            start=True, stop=True))
                        rb1_ps = CT.n

                        @blk.vector
                        def _(v):
                            v.wait_ge(CT.h, rb1_ps)
                            CV.inc(v.tensor_copy(rbb[0:64, :], P[0][0:64, :]))
                            v.wait_ge(CV.h, CV.n)
                            CV.inc(v.tensor_tensor(c1s[0:64, :], P[5][0:64, :], rbb[0:64, :],
                                                   OP.mult))
                        c1_done = CV.n
                        pfree[5] = [(CV, c1_done)]

                        @blk.tensor
                        def _(t):
                            t.wait_ge(CV.h, c1_done)
                            CT.inc(t.matmul(P[0][64:128, :], idn[0:64, 0:64], c1s[0:64, :],
                                            start=True, stop=True, tile_position=(0, 64)))
                        mv_ps = CT.n

                        @blk.vector
                        def _(v):
                            v.wait_ge(CT.h, mv_ps)
                            CV.inc(v.tensor_copy(cc[64:128, ccl:ccl + 512], P[0][64:128, :]))
                        cc_all = CV.n
                        pfree[0] = [(CV, CV.n)]

                for tt in range(NT):
                    ctl = [0, 0]
                    for dh in range(2):
                        @blk.tensor
                        def _(t):
                            t.wait_ge(CG.h, g_wo)
                            t.wait_ge(CV.h, cc_all)
                            waits(t, pfree[dh])
                            for c in range(8):
                                inst = t.matmul(P[dh][:, :],
                                                cc[:, c * S + tt * 128:c * S + (tt + 1) * 128],
                                                wos[:, c * 1024 + dh * 512:c * 1024 + (dh + 1) * 512],
                                                start=(c == 0), stop=(not causal and c == 7))
                            if causal:
                                inst = t.matmul(P[dh][:, :], ones[0:1, 0:128],
                                                bor[0:1, dh * 512:(dh + 1) * 512],
                                                start=False, stop=True)
                            CT.inc(inst)
                        ctl[dh] = CT.n

                    boundary_tile(k, tt, "psum", (0, 1, ctl[0], ctl[1]), is_last)
                if not is_last:
                    xln_ready = CX.n

        def emit_ffn(k, is_last):
            nonlocal xln_ready
            sub_start_cx = xln_ready
            with ExitStack() as fes:
                A = fes.enter_context
                xt = [A(nc.sbuf_tensor(f"s{k}_xt{i}", [128, 8 * TB], BF16)) for i in range(2)]
                u = A(nc.sbuf_tensor(f"s{k}_u", [128, 32 * TB], BF16))
                acc = A(nc.sbuf_tensor(f"s{k}_acc", [128, 8 * 1024], F32))
                w1b = [A(nc.sbuf_tensor(f"s{k}_w1b{i}", [128, 2048], BF16)) for i in range(2)]
                w2b = [A(nc.sbuf_tensor(f"s{k}_w2b{i}", [128, 512], BF16)) for i in range(2)]
                me = [A(nc.sbuf_tensor(f"s{k}_me{i}", [128, TB], BF16)) for i in range(2)]
                sg = [A(nc.sbuf_tensor(f"s{k}_sg{i}", [128, 512], BF16)) for i in range(2)]
                tsc = A(nc.sbuf_tensor(f"s{k}_tsc", [128, 512], F32))

                xt_free = [[], []]
                w1_free = [[(CX, sub_start_cx)], [(CX, sub_start_cx)]]
                w2_free = [[(CX, sub_start_cx)], [(CX, sub_start_cx)]]
                me_free = [[], []]
                me_ready = [0, 0]
                prev_y_done = 0
                idxb_free = []
                mrow_prev = 0

                for tb in range(NTB):
                    tpar = tb % 2

                    @blk.scalar
                    def _(sq):
                        waits(sq, xt_free[tpar])
                        sq.wait_ge(CX.h, xln_ready)
                        for c in range(8):
                            CD.dma(sq.dma_start(
                                out=xt[tpar][:, c * TB:(c + 1) * TB],
                                in_=xln_d[tb * TB:(tb + 1) * TB, c * 128:(c + 1) * 128],
                                transpose=True))
                    xt_ld = CD.n

                    for tt8 in range(TB // 128):
                        tok0 = tb * TB + tt8 * 128

                        @blk.tensor
                        def _(t):
                            t.wait_ge(CD.h, xt_ld)
                            t.wait_ge(CG.h, g_consts)
                            waits(t, pfree[4])
                            for c in range(8):
                                t.matmul(P[4][:, 0:8],
                                         xt[tpar][:, c * TB + tt8 * 128:c * TB + (tt8 + 1) * 128],
                                         rws[:, c * 8:(c + 1) * 8],
                                         start=(c == 0), stop=False)
                            CT.inc(t.matmul(P[4][:, 0:8], ones[0:1, 0:128], rbs[0:1, :],
                                            start=False, stop=True))
                        rt_ps = CT.n

                        @blk.vector
                        def _(v):
                            v.wait_ge(CT.h, rt_ps)
                            CV.inc(v.reduce_max(r8[:, 0:1], P[4][:, 0:8], axis=AX))
                            v.wait_ge(CV.h, CV.n)
                            CV.inc(v.tensor_copy(r8b[:, :], bass.AP(r8, 0, [[9, 128], [0, 8]])))
                            v.wait_ge(CV.h, CV.n)
                            CV.inc(v.max_index(idx8[:, :], r8b[:, :], P[4][:, 0:8]))
                            v.wait_ge(CV.h, CV.n)
                            waits(v, idxb_free)
                            CV.inc(v.tensor_copy(idxb[:, :], idx8[:, 0:1]))
                        idx_done = CV.n
                        pfree[4] = [(CV, idx_done)]

                        @blk.sync
                        def _(sy):
                            sy.wait_ge(CV.h, idx_done)
                            CX.dma(sy.dma_start(
                                out=idx_d[0:1, tok0:tok0 + 128].rearrange("one t -> t one"),
                                in_=idxb[:, :]))
                        idxb_free = [(CX, CX.n)]
                    idx_stores = CX.n

                    @blk.sync
                    def _(sy):
                        sy.wait_ge(CX.h, idx_stores)
                        sy.wait_ge(CV.h, mrow_prev)
                        CX.dma(sy.dma_start(out=idxT[0:1, :],
                                            in_=idx_d[0:1, tb * TB:(tb + 1) * TB]))
                    idxT_ld = CX.n

                    def emit_me(e):
                        nonlocal mrow_prev
                        mpar = e % 2

                        @blk.vector
                        def _(v):
                            v.wait_ge(CX.h, idxT_ld)
                            v.wait_ge(CT.h, prev_y_done)
                            CV.inc(v.tensor_single_scalar(mrow[mpar][0:1, :], idxT[0:1, :],
                                                          float(e), OP.is_equal))
                        mrow_done = CV.n
                        mrow_prev = mrow_done

                        @blk.tensor
                        def _(t):
                            t.wait_ge(CV.h, mrow_done)
                            waits(t, pfree[4])
                            CT.inc(t.matmul(P[4][:, :], ones[0:1, 0:128],
                                            mrow[mpar][0:1, 0:512], start=True, stop=True))
                            waits(t, pfree[5])
                            CT.inc(t.matmul(P[5][:, :], ones[0:1, 0:128],
                                            mrow[mpar][0:1, 512:TB],
                                            start=True, stop=True))
                        ps4 = CT.n - 1
                        ps5 = CT.n

                        @blk.vector
                        def _(v):
                            waits(v, me_free[mpar])
                            v.wait_ge(CT.h, ps4)
                            CV.inc(v.tensor_copy(me[mpar][:, 0:512], P[4][:, :]))
                            v.wait_ge(CT.h, ps5)
                            CV.inc(v.tensor_copy(me[mpar][:, 512:1024], P[5][:, :]))
                        pfree[4] = [(CV, CV.n - 1)]
                        pfree[5] = [(CV, CV.n)]
                        me_ready[mpar] = CV.n

                    emit_me(0)

                    for e in range(E):
                        mpar = e % 2
                        h_done = {}
                        w1_marks = {}

                        def w1_load(j):
                            wpar = j % 2
                            @blk.gpsimd
                            def _(g):
                                waits(g, w1_free[wpar])
                                CG.dma(g.dma_start(out=w1b[wpar][:, :], in_=w1h[e, j]))
                            w1_marks[j] = CG.n

                        w1_load(0)
                        w1_load(1)

                        first_u = True
                        for j in range(32):
                            wpar = j % 2
                            for sb in range(2):
                                ba = 0 if j % 2 == 0 else 2
                                bg = ba + 1

                                @blk.tensor
                                def _(t):
                                    t.wait_ge(CG.h, w1_marks[j])
                                    if first_u:
                                        t.wait_ge(CT.h, prev_y_done)
                                    waits(t, pfree[ba])
                                    for c in range(8):
                                        inst = t.matmul(
                                            P[ba][:, :], w1b[wpar][:, c * 128:(c + 1) * 128],
                                            xt[tpar][:, c * TB + sb * 512:c * TB + (sb + 1) * 512],
                                            start=(c == 0), stop=(c == 7))
                                    CT.inc(inst)
                                    waits(t, pfree[bg])
                                    for c in range(8):
                                        inst = t.matmul(
                                            P[bg][:, :], w1b[wpar][:, 1024 + c * 128:1024 + (c + 1) * 128],
                                            xt[tpar][:, c * TB + sb * 512:c * TB + (sb + 1) * 512],
                                            start=(c == 0), stop=(c == 7))
                                    CT.inc(inst)
                                a_ps = CT.n - 1
                                g_ps = CT.n
                                if sb == 1:
                                    h_done[j] = CT.n
                                    if j + 2 < 32:
                                        w1_free[wpar] = [(CT, CT.n)]
                                        w1_load(j + 2)

                                spar = (2 * j + sb) % 2
                                bcol = e * 64 + j
                                gcol = e * 64 + 32 + j

                                @blk.scalar
                                def _(sq):
                                    sq.wait_ge(CT.h, g_ps)
                                    waits(sq, sg_free[spar])
                                    CS.inc(sq.activation(sg[spar][:, :], P[bg][:, :], AF.Sigmoid,
                                                         bias=b1s[:, gcol:gcol + 1]))
                                sg_done = CS.n
                                pfree[bg] = [(CS, sg_done)]

                                @blk.vector
                                def _(v):
                                    v.wait_ge(CT.h, a_ps)
                                    v.wait_ge(CS.h, sg_done)
                                    if first_u:
                                        v.wait_ge(CV.h, me_ready[mpar])
                                        v.wait_ge(CT.h, prev_y_done)
                                    CV.inc(v.scalar_tensor_tensor(tsc[:, :], P[ba][:, :],
                                                                  b1s[:, bcol:bcol + 1],
                                                                  sg[spar][:, :], OP.add, OP.mult))
                                    CV.inc(v.tensor_tensor(
                                        u[:, j * TB + sb * 512:j * TB + (sb + 1) * 512],
                                        tsc[:, :], me[mpar][:, sb * 512:(sb + 1) * 512], OP.mult))
                                pfree[ba] = [(CV, CV.n - 1)]
                                sg_free[spar] = [(CV, CV.n)]
                                first_u = False
                        u_done = CV.n
                        w1_free[0] = [(CT, h_done[30])]
                        w1_free[1] = [(CT, h_done[31])]
                        me_free[mpar] = [(CV, u_done)]

                        if e + 1 < E:
                            emit_me(e + 1)

                        w2_marks = {}
                        w2_cons = {}

                        def w2_load(seq):
                            dh_, j_ = divmod(seq, 32)
                            wpar2 = seq % 2
                            @blk.gpsimd
                            def _(g):
                                if seq >= 2:
                                    g.wait_ge(CT.h, w2_cons[seq - 2])
                                else:
                                    waits(g, w2_free[wpar2])
                                CG.dma(g.dma_start(out=w2b[wpar2][:, :], in_=w2h[e, dh_, j_]))
                            w2_marks[seq] = CG.n

                        w2_load(0)
                        w2_load(1)

                        y_ts_marks = [0] * 8
                        for dh in range(2):
                            @blk.tensor
                            def _(t):
                                for ts in range(8):
                                    waits(t, pfree[ts])
                                    inst = t.matmul(
                                        P[ts][:, :],
                                        mrow[e % 2][0:1, ts * 128:(ts + 1) * 128],
                                        b2r[0:1, e * 1024 + dh * 512:e * 1024 + (dh + 1) * 512],
                                        start=True, stop=False)
                                CT.inc(inst)

                            for j in range(32):
                                seq = dh * 32 + j
                                wpar2 = seq % 2

                                @blk.tensor
                                def _(t):
                                    t.wait_ge(CG.h, w2_marks[seq])
                                    t.wait_ge(CV.h, u_done)
                                    for ts in range(8):
                                        inst = t.matmul(
                                            P[ts][:, :],
                                            u[:, j * TB + ts * 128:j * TB + (ts + 1) * 128],
                                            w2b[wpar2][:, :], start=False, stop=(j == 31))
                                        if j == 31:
                                            CT.inc(inst)
                                    if j != 31:
                                        CT.inc(inst)
                                w2_cons[seq] = CT.n
                                if seq + 2 < 64:
                                    w2_load(seq + 2)
                                if j == 31:
                                    for ts in range(8):
                                        y_ts_marks[ts] = CT.n - 7 + ts

                            @blk.vector
                            def _(v):
                                for ts in range(8):
                                    v.wait_ge(CT.h, y_ts_marks[ts])
                                    dst = acc[:, ts * 1024 + dh * 512:ts * 1024 + (dh + 1) * 512]
                                    if e == 0:
                                        CV.inc(v.tensor_copy(dst, P[ts][:, :]))
                                    else:
                                        CV.inc(v.tensor_tensor(dst, P[ts][:, :], dst, OP.add))
                            for ts in range(8):
                                pfree[ts] = [(CV, CV.n - 7 + ts)]
                            if e == E - 1 and dh == 1:
                                acc_ready = [CV.n - 7 + ts for ts in range(8)]
                        w2_free[0] = [(CT, w2_cons[62])]
                        w2_free[1] = [(CT, w2_cons[63])]
                        prev_y_done = CT.n

                    for ts in range(8):
                        tt = tb * 8 + ts
                        boundary_tile(k, tt, "sbuf", (acc, ts * 1024, acc_ready[ts]), is_last)
                    xt_free[tpar] = [(CT, prev_y_done)]
                if not is_last:
                    xln_ready = CX.n

        sg_free = [[], []]

        for k in range(n_sub):
            kind = SUBLAYERS[k]
            is_last = (k == n_sub - 1)
            if kind == "ffn":
                emit_ffn(k, is_last)
            else:
                emit_attn(k, "a" if kind == "attn_a" else "b", is_last)

        final_cx = CX.n

        @blk.sync
        def _(sy):
            sy.wait_ge(CX.h, final_cx)

    return nc



def _prep_weights(ins):
    f32 = np.float32

    def c(x):
        return np.ascontiguousarray(np.asarray(x, dtype=f32))

    wq, wk, wv = ins["L0_attn_a_Wq"], ins["L0_attn_a_Wk"], ins["L0_attn_a_Wv"]
    qkv = np.stack([np.asarray(wq), np.asarray(wk), np.asarray(wv)], axis=1)
    qkv = qkv.reshape(8, 2, 3, 8, 128, 64)
    qkv = qkv.transpose(0, 4, 2, 3, 1, 5)
    wqkv_a = c(qkv.reshape(8, 128, 3072))

    kb = np.asarray(ins["L0_attn_b_kernel"])
    qkvb = np.transpose(kb, (1, 0, 2, 3)).reshape(8, 2, 3, 8, 128, 64)
    qkvb = qkvb.transpose(0, 4, 2, 3, 1, 5)
    wqkv_b = c(qkvb.reshape(8, 128, 3072))

    wo = np.asarray(ins["L0_attn_a_Wo"])
    wo_a = c(wo.reshape(8, 128, 1024).transpose(1, 0, 2).reshape(128, 8192))
    wob = np.asarray(ins["L0_attn_b_WO"])[0]
    wob_r = wob.reshape(64, 16, 1024).transpose(1, 0, 2).reshape(1024, 1024)
    wo_b = c(wob_r.reshape(8, 128, 1024).transpose(1, 0, 2).reshape(128, 8192))

    bq = np.asarray(ins["L0_attn_a_bq"]).reshape(8, 2, 64).transpose(1, 2, 0).reshape(128, 8)
    bk = np.asarray(ins["L0_attn_a_bk"]).reshape(8, 2, 64).transpose(1, 2, 0).reshape(128, 8)
    bqk_a = c(np.concatenate([bq, bk], axis=1))
    bv_a = c(np.asarray(ins["L0_attn_a_bv"]).reshape(1, 1024))
    bo_a = c(np.asarray(ins["L0_attn_a_bo"]).reshape(1, 1024))

    w1 = np.asarray(ins["L0_ffn_w1"])
    w1h = c(w1.reshape(8, 8, 128, 2, 32, 128).transpose(0, 4, 2, 3, 1, 5)
            .reshape(E, 32, 128, 2048))
    w2 = np.asarray(ins["L0_ffn_w2"])
    w2h = c(w2.reshape(8, 32, 128, 2, 512).transpose(0, 3, 1, 2, 4))
    b1 = np.asarray(ins["L0_ffn_b1"])
    b1h = c(b1.reshape(8, 2, 32, 128).transpose(3, 0, 1, 2).reshape(128, E * 64))
    b2h = c(np.asarray(ins["L0_ffn_b2"]).reshape(1, E * 1024))
    rwh = c(np.asarray(ins["L0_ffn_router_w"]).reshape(8, 128, 8).transpose(1, 0, 2)
            .reshape(128, 64))
    rbh = c(np.asarray(ins["L0_ffn_router_b"]).reshape(1, 8))

    pp = np.arange(128)[:, None]
    ff = np.arange(512)[None, :]
    cmask = c(np.concatenate([((pp + 128 * r) <= ff).astype(f32) for r in range(4)], axis=1))
    idn_h = c(np.eye(128, dtype=f32))

    return dict(wqkv_a=wqkv_a, wqkv_b=wqkv_b, wo_a=wo_a, wo_b=wo_b, bqk_a=bqk_a,
                bv_a=bv_a, bo_a=bo_a, w1h=w1h, w2h=w2h, b1h=b1h, b2h=b2h,
                rwh=rwh, rbh=rbh, cmask=cmask, idn_h=idn_h)


def kernel(**inputs):
    emb = np.ascontiguousarray(np.asarray(inputs["L0_attn_a_embeddings"], dtype=np.float32))
    w = _prep_weights(inputs)
    nc = build_nc(4)
    in_maps = []
    for core in range(8):
        m = dict(w)
        m["x0"] = np.ascontiguousarray(emb[core])
        in_maps.append(m)
    res = run_bass_kernel_spmd(nc, in_maps, core_ids=list(range(8)))
    outp = np.stack([res.results[c]["out"] for c in range(8)], axis=0)
    return outp.astype(np.float32)


# version 02
# speedup vs baseline: 1.2351x; 1.2351x over previous
import numpy as np
import ml_dtypes
from contextlib import ExitStack

import concourse.bass as bass
import concourse.mybir as mybir
from concourse.bass_utils import run_bass_kernel_spmd

F32 = mybir.dt.float32
BF16 = mybir.dt.bfloat16
U32 = mybir.dt.uint32
AF = mybir.ActivationFunctionType
OP = mybir.AluOpType
AX = mybir.AxisListType.X

S = 4096
D = 1024
H = 16
NP = 8
DK = 64
E = 8
TB = 1024
NTB = S // TB
NT = S // 128
NQB = S // 512
EPS = 1e-5

SUBLAYERS = ("attn_a", "ffn", "attn_b", "ffn")


class Ctr:
    def __init__(self, nc, es, name):
        self.h = es.enter_context(nc.semaphore(name))
        self.n = 0

    def inc(self, inst, k=1):
        inst.then_inc(self.h, k)
        self.n += k
        return self.n

    def dma(self, inst):
        return self.inc(inst, 16)


def build_nc(n_sub=4):
    nc = bass.Bass("TRN2", target_bir_lowering=False)

    x0 = nc.dram_tensor("x0", [S, D], F32, kind="ExternalInput").ap()
    out = nc.dram_tensor("out", [S, D], F32, kind="ExternalOutput").ap()
    wqkv_a = nc.dram_tensor("wqkv_a", [NP, 128, 3072], F32, kind="ExternalInput").ap()
    wqkv_b = nc.dram_tensor("wqkv_b", [NP, 128, 3072], F32, kind="ExternalInput").ap()
    wo_a = nc.dram_tensor("wo_a", [128, 8192], F32, kind="ExternalInput").ap()
    wo_b = nc.dram_tensor("wo_b", [128, 8192], F32, kind="ExternalInput").ap()
    bqk_a = nc.dram_tensor("bqk_a", [128, 16], F32, kind="ExternalInput").ap()
    bv_a = nc.dram_tensor("bv_a", [1, 1024], F32, kind="ExternalInput").ap()
    bo_a = nc.dram_tensor("bo_a", [1, 1024], F32, kind="ExternalInput").ap()
    cmask = nc.dram_tensor("cmask", [128, 2048], F32, kind="ExternalInput").ap()
    idn_h = nc.dram_tensor("idn_h", [128, 128], F32, kind="ExternalInput").ap()
    w1h = nc.dram_tensor("w1h", [E, 32, 128, 2048], F32, kind="ExternalInput").ap()
    w2h = nc.dram_tensor("w2h", [E, 2, 32, 128, 512], F32, kind="ExternalInput").ap()
    b1h = nc.dram_tensor("b1h", [128, E * 64], F32, kind="ExternalInput").ap()
    b2h = nc.dram_tensor("b2h", [1, E * 1024], F32, kind="ExternalInput").ap()
    rwh = nc.dram_tensor("rwh", [128, 64], F32, kind="ExternalInput").ap()
    rbh = nc.dram_tensor("rbh", [1, 8], F32, kind="ExternalInput").ap()
    xresA = nc.dram_tensor("xresA", [S, D], F32, kind="Internal").ap()
    xresB = nc.dram_tensor("xresB", [S, D], F32, kind="Internal").ap()
    xln_d = nc.dram_tensor("xln_d", [S, D], BF16, kind="Internal").ap()
    idx_d = nc.dram_tensor("idx_d", [1, S], BF16, kind="Internal").ap()

    res_chain = [x0, xresA, xresB, xresA]
    res_dst = [xresA, xresB, xresA, None]

    with ExitStack() as es:
        EC = es.enter_context
        P = [EC(nc.psum_tensor(f"P{i}", [128, 512], F32)) for i in range(8)]
        xr = [EC(nc.sbuf_tensor(f"xr{i}", [128, D], F32)) for i in range(2)]
        xnw = [EC(nc.sbuf_tensor(f"xnw{i}", [128, D], F32)) for i in range(2)]
        xlb = [EC(nc.sbuf_tensor(f"xlb{i}", [128, D], BF16)) for i in range(2)]
        sqd = EC(nc.sbuf_tensor("sqd", [128, D], F32))
        stt = [EC(nc.sbuf_tensor(f"stt{i}", [128, 8], F32)) for i in range(2)]
        ones = EC(nc.sbuf_tensor("ones", [1, 512], BF16))
        cm = EC(nc.sbuf_tensor("cm", [128, 2048], BF16))
        bqk = EC(nc.sbuf_tensor("bqk", [128, 16], F32))
        bvr = EC(nc.sbuf_tensor("bvr", [1, 1024], BF16))
        bor = EC(nc.sbuf_tensor("bor", [1, 1024], BF16))
        b2r = EC(nc.sbuf_tensor("b2r", [1, E * 1024], BF16))
        b1s = EC(nc.sbuf_tensor("b1s", [128, E * 64], F32))
        rws = EC(nc.sbuf_tensor("rws", [128, 64], BF16))
        rbs = EC(nc.sbuf_tensor("rbs", [1, 8], BF16))
        idn = EC(nc.sbuf_tensor("idn", [128, 128], BF16))
        r8 = EC(nc.sbuf_tensor("r8", [128, 9], F32))
        r8b = EC(nc.sbuf_tensor("r8b", [128, 8], F32))
        idx8 = EC(nc.sbuf_tensor("idx8", [128, 8], U32))
        idxb = EC(nc.sbuf_tensor("idxb", [128, 1], BF16))
        idxT = EC(nc.sbuf_tensor("idxT", [1, TB], BF16))
        mrow = [EC(nc.sbuf_tensor(f"mrow{i}", [1, TB], BF16)) for i in range(2)]
        recs = EC(nc.sbuf_tensor("recs", [1, 512], BF16))
        rbb = EC(nc.sbuf_tensor("rbb", [128, 512], BF16))
        c1s = EC(nc.sbuf_tensor("c1s", [128, 512], BF16))
        epsc = EC(nc.sbuf_tensor("epsc", [128, 1], F32))

        CT = Ctr(nc, es, "CT")
        CV = Ctr(nc, es, "CV")
        CS = Ctr(nc, es, "CS")
        CG = Ctr(nc, es, "CG")
        CX = Ctr(nc, es, "CX")
        CD = Ctr(nc, es, "CD")

        blk = EC(nc.Block())

        def waits(eng, conds):
            for c, v in conds:
                if c is not None and v > 0:
                    eng.wait_ge(c.h, v)

        pfree = [[] for _ in range(8)]

        @blk.gpsimd
        def _(g):
            CG.dma(g.dma_start(out=cm[:, :], in_=cmask))
            CG.dma(g.dma_start(out=bqk[:, :], in_=bqk_a))
            CG.dma(g.dma_start(out=bvr[:, :], in_=bv_a))
            CG.dma(g.dma_start(out=bor[:, :], in_=bo_a))
            CG.dma(g.dma_start(out=b2r[:, :], in_=b2h))
            CG.dma(g.dma_start(out=b1s[:, :], in_=b1h))
            CG.dma(g.dma_start(out=rws[:, :], in_=rwh))
            CG.dma(g.dma_start(out=rbs[:, :], in_=rbh))
            CG.dma(g.dma_start(out=idn[:, :], in_=idn_h))
        g_consts = CG.n

        @blk.vector
        def _(v):
            v.memset(epsc[:, :], EPS)
            CV.inc(v.memset(ones[:, :], 1.0))
        v_ones = CV.n

        bst = {"xr": [[], []], "xnw": [[], []], "xlb": [[], []], "st": [[], []]}

        def ln_chain(par, src_done_list, tag_last):
            st = stt[par]

            @blk.vector
            def _(v):
                waits(v, src_done_list)
                waits(v, bst["st"][par])
                v.reduce_sum(st[:, 0:1], xnw[par][:, :], axis=AX)
                v.tensor_tensor(sqd[:, :], xnw[par][:, :], xnw[par][:, :], OP.mult)
                CV.inc(v.reduce_sum(st[:, 1:2], sqd[:, :], axis=AX))
            sum_done = CV.n
            sq_done = 0

            @blk.vector
            def _(v):
                v.wait_ge(CV.h, sum_done)
                CV.inc(v.tensor_scalar_mul(st[:, 2:3], st[:, 0:1], -1.0 / D))
                v.wait_ge(CV.h, CV.n)
                v.tensor_tensor(st[:, 3:4], st[:, 2:3], st[:, 2:3], OP.mult)
                CV.inc(v.tensor_scalar_mul(st[:, 4:5], st[:, 1:2], 1.0 / D))
                v.wait_ge(CV.h, CV.n)
                CV.inc(v.tensor_tensor(st[:, 5:6], st[:, 4:5], st[:, 3:4], OP.subtract))
            var_done = CV.n

            @blk.scalar
            def _(s):
                s.wait_ge(CV.h, var_done)
                CS.inc(s.activation(st[:, 6:7], st[:, 5:6], AF.Sqrt, bias=epsc[:, 0:1]))
            sd_done = CS.n

            @blk.vector
            def _(v):
                v.wait_ge(CS.h, sd_done)
                waits(v, bst["xlb"][par])
                CV.inc(v.reciprocal(st[:, 7:8], st[:, 6:7]))
                v.wait_ge(CV.h, CV.n)
                CV.inc(v.tensor_scalar(xlb[par][:, :], xnw[par][:, :], st[:, 2:3],
                                       st[:, 7:8], OP.add, OP.mult))
            xlb_done = CV.n
            bst["st"][par] = [(CV, xlb_done), (CS, sd_done)]
            return xlb_done, sq_done

        def boundary_tile(k, tt, src_mode, src_info, is_last):
            par = tt % 2
            rsrc = res_chain[k]
            rdst = out if is_last else res_dst[k]

            @blk.sync
            def _(sy):
                waits(sy, bst["xr"][par])
                CX.dma(sy.dma_start(out=xr[par][:, :], in_=rsrc[tt * 128:(tt + 1) * 128, :]))
            x_loaded = CX.n

            @blk.vector
            def _(v):
                v.wait_ge(CX.h, x_loaded)
                waits(v, bst["xnw"][par])
                if src_mode == "psum":
                    blo, bhi, ctlo, cthi = src_info
                    v.wait_ge(CT.h, ctlo)
                    v.tensor_tensor(xnw[par][:, 0:512], P[blo][:, :], xr[par][:, 0:512], OP.add)
                    v.wait_ge(CT.h, cthi)
                    CV.inc(v.tensor_tensor(xnw[par][:, 512:1024], P[bhi][:, :],
                                           xr[par][:, 512:1024], OP.add))
                else:
                    acc, coff, cvr = src_info
                    v.wait_ge(CV.h, cvr)
                    CV.inc(v.tensor_tensor(xnw[par][:, :], acc[:, coff:coff + D],
                                           xr[par][:, :], OP.add))
            xnew_done = CV.n
            bst["xr"][par] = [(CV, xnew_done)]
            if src_mode == "psum":
                pfree[src_info[0]] = [(CV, xnew_done)]
                pfree[src_info[1]] = [(CV, xnew_done)]

            @blk.sync
            def _(sy):
                sy.wait_ge(CV.h, xnew_done)
                CX.dma(sy.dma_start(out=rdst[tt * 128:(tt + 1) * 128, :], in_=xnw[par][:, :]))
            xnew_stored = CX.n

            if is_last:
                bst["xnw"][par] = [(CX, xnew_stored)]
                return xnew_done

            xlb_done, sq_done = ln_chain(par, [], is_last)
            bst["xnw"][par] = [(CX, xnew_stored), (CV, xlb_done)]

            @blk.sync
            def _(sy):
                sy.wait_ge(CV.h, xlb_done)
                CX.dma(sy.dma_start(out=xln_d[tt * 128:(tt + 1) * 128, :], in_=xlb[par][:, :]))
            bst["xlb"][par] = [(CX, CX.n)]
            return xnew_done

        for tt in range(NT):
            par = tt % 2

            @blk.sync
            def _(sy):
                waits(sy, bst["xnw"][par])
                CX.dma(sy.dma_start(out=xnw[par][:, :], in_=x0[tt * 128:(tt + 1) * 128, :]))
            x_loaded = CX.n

            xlb_done, sq_done = ln_chain(par, [(CX, x_loaded)], False)
            bst["xnw"][par] = [(CV, xlb_done)]

            @blk.sync
            def _(sy):
                sy.wait_ge(CV.h, xlb_done)
                CX.dma(sy.dma_start(out=xln_d[tt * 128:(tt + 1) * 128, :], in_=xlb[par][:, :]))
            bst["xlb"][par] = [(CX, CX.n)]

        xln_ready = CX.n

        def emit_attn(k, kind, is_last):
            nonlocal xln_ready
            causal = (kind == "a")
            wqkv = wqkv_a if causal else wqkv_b
            wo = wo_a if causal else wo_b
            scale = 0.125 if causal else 1.0 / 65536.0
            sub_start_cx = xln_ready

            def off_w(s_, c, hh):
                return ((s_ * 8 + c) * 2 + hh) * 64

            with ExitStack() as aes:
                A = aes.enter_context
                xs = [A(nc.sbuf_tensor(f"s{k}_xs{i}", [128, 4096], BF16)) for i in range(2)]
                qt = A(nc.sbuf_tensor(f"s{k}_qt", [128, S], BF16))
                ktb = A(nc.sbuf_tensor(f"s{k}_ktb", [128, S], BF16))
                vsb = [A(nc.sbuf_tensor(f"s{k}_vsb{i}", [128, NT * 65], BF16)) for i in range(2)]
                pt = [A(nc.sbuf_tensor(f"s{k}_pt{i}", [128, 512], BF16)) for i in range(4)]
                wq = [A(nc.sbuf_tensor(f"s{k}_wq{i}", [128, 3072], BF16)) for i in range(2)]
                wos = A(nc.sbuf_tensor(f"s{k}_wos", [128, 8192], BF16))
                cc = A(nc.sbuf_tensor(f"s{k}_cc", [128, 8 * S], BF16))

                @blk.gpsimd
                def _(g):
                    g.wait_ge(CX.h, sub_start_cx)
                    CG.dma(g.dma_start(out=wos[:, :], in_=wo))
                g_wo = CG.n

                @blk.vector
                def _(v):
                    for i in range(2):
                        va = bass.AP(vsb[i], 64, [[NT * 65, 128], [65, NT]])
                        inst = v.memset(va, 1.0)
                    CV.inc(inst)
                v_vones = CV.n

                xs_free = [[], []]
                wq_free = [[], []]
                pt_free = [[]] * 4
                cc_all = 0
                pending_evac = None

                def emit_evac(qb, c0, c1, pv_mark_last):
                    nonlocal p_cur
                    ccl = p_cur * S + qb * 512

                    @blk.vector
                    def _(v):
                        v.wait_ge(CT.h, pv_mark_last)
                        waits(v, pfree[0])
                        with nc.allow_low_precision(reason="bf16 matmul operand; fp32 accumulation"):
                            CV.inc(v.reciprocal(recs[0:1, :], P[c0][64:65, :]))
                    rec0 = CV.n

                    @blk.tensor
                    def _(t):
                        t.wait_ge(CV.h, rec0)
                        CT.inc(t.matmul(P[0][0:64, :], ones[0:1, 0:64], recs[0:1, :],
                                        start=True, stop=True))
                    rb0_ps = CT.n

                    @blk.vector
                    def _(v):
                        v.wait_ge(CT.h, rb0_ps)
                        CV.inc(v.tensor_copy(rbb[0:64, :], P[0][0:64, :]))
                        v.wait_ge(CV.h, CV.n)
                        CV.inc(v.tensor_tensor(cc[0:64, ccl:ccl + 512], P[c0][0:64, :],
                                               rbb[0:64, :], OP.mult))
                        with nc.allow_low_precision(reason="bf16 matmul operand; fp32 accumulation"):
                            CV.inc(v.reciprocal(recs[0:1, :], P[c1][64:65, :]))
                    cc0_done = CV.n - 1
                    rec1 = CV.n
                    pfree[c0] = [(CV, cc0_done)]

                    @blk.tensor
                    def _(t):
                        t.wait_ge(CV.h, rec1)
                        CT.inc(t.matmul(P[0][0:64, :], ones[0:1, 0:64], recs[0:1, :],
                                        start=True, stop=True))
                    rb1_ps = CT.n

                    @blk.vector
                    def _(v):
                        v.wait_ge(CT.h, rb1_ps)
                        CV.inc(v.tensor_copy(rbb[0:64, :], P[0][0:64, :]))
                        v.wait_ge(CV.h, CV.n)
                        CV.inc(v.tensor_tensor(c1s[0:64, :], P[c1][0:64, :], rbb[0:64, :],
                                               OP.mult))
                    c1_done = CV.n
                    pfree[c1] = [(CV, c1_done)]

                    @blk.tensor
                    def _(t):
                        t.wait_ge(CV.h, c1_done)
                        CT.inc(t.matmul(P[0][64:128, :], idn[0:64, 0:64], c1s[0:64, :],
                                        start=True, stop=True, tile_position=(0, 64)))
                    mv_ps = CT.n

                    @blk.vector
                    def _(v):
                        v.wait_ge(CT.h, mv_ps)
                        CV.inc(v.tensor_copy(cc[64:128, ccl:ccl + 512], P[0][64:128, :]))
                    pfree[0] = [(CV, CV.n)]
                    return CV.n

                for p in range(NP):
                    p_cur = p
                    wpar = p % 2

                    @blk.gpsimd
                    def _(g):
                        waits(g, wq_free[wpar])
                        CG.dma(g.dma_start(out=wq[wpar][:, :], in_=wqkv[p]))
                    g_w = CG.n

                    last_vps = 0
                    for qb in range(NQB):
                        xpar = qb % 2

                        @blk.sync
                        def _(sq):
                            waits(sq, xs_free[xpar])
                            sq.wait_ge(CX.h, xln_ready)
                            for c in range(8):
                                CD.dma(sq.dma_start(
                                    out=xs[xpar][:, c * 512:(c + 1) * 512],
                                    in_=xln_d[qb * 512:(qb + 1) * 512, c * 128:(c + 1) * 128],
                                    transpose=True))
                        x_ld = CD.n

                        @blk.tensor
                        def _(t):
                            t.wait_ge(CG.h, g_w)
                            t.wait_ge(CD.h, x_ld)
                            waits(t, pfree[0])
                            for hh in range(2):
                                for c in range(8):
                                    woff = off_w(0, c, hh)
                                    tp = None if hh == 0 else (0, 64)
                                    inst = t.matmul(P[0][hh * 64:(hh + 1) * 64, :],
                                                    wq[wpar][:, woff:woff + 64],
                                                    xs[xpar][:, c * 512:(c + 1) * 512],
                                                    start=(c == 0), stop=(c == 7),
                                                    tile_position=tp)
                            CT.inc(inst)
                            waits(t, pfree[1])
                            for hh in range(2):
                                for c in range(8):
                                    woff = off_w(1, c, hh)
                                    tp = None if hh == 0 else (0, 64)
                                    inst = t.matmul(P[1][hh * 64:(hh + 1) * 64, :],
                                                    wq[wpar][:, woff:woff + 64],
                                                    xs[xpar][:, c * 512:(c + 1) * 512],
                                                    start=(c == 0), stop=(c == 7),
                                                    tile_position=tp)
                            CT.inc(inst)
                            waits(t, pfree[2])
                            t.wait_ge(CV.h, v_vones)
                            for sub in range(4):
                                for c in range(8):
                                    inst = t.matmul(
                                        P[2][:, sub * 128:(sub + 1) * 128],
                                        xs[xpar][:, c * 512 + sub * 128:c * 512 + (sub + 1) * 128],
                                        wq[wpar][:, off_w(2, c, 0):off_w(2, c, 0) + 128],
                                        start=(c == 0), stop=(c == 7 and not causal))
                                if causal:
                                    inst = t.matmul(P[2][:, sub * 128:(sub + 1) * 128],
                                                    ones[0:1, 0:128],
                                                    bvr[0:1, p * 128:(p + 1) * 128],
                                                    start=False, stop=True)
                            CT.inc(inst)
                        qt_ps = CT.n - 2
                        kt_ps = CT.n - 1
                        v_ps = CT.n
                        last_vps = v_ps

                        @blk.vector
                        def _(v):
                            v.wait_ge(CT.h, qt_ps)
                            if causal:
                                CV.inc(v.tensor_scalar_add(qt[:, qb * 512:(qb + 1) * 512],
                                                           P[0][:, :], bqk[:, p:p + 1]))
                            else:
                                CV.inc(v.tensor_copy(qt[:, qb * 512:(qb + 1) * 512], P[0][:, :]))
                            v.wait_ge(CT.h, kt_ps)
                            if causal:
                                CV.inc(v.tensor_scalar_add(ktb[:, qb * 512:(qb + 1) * 512],
                                                           P[1][:, :], bqk[:, 8 + p:9 + p]))
                            else:
                                CV.inc(v.tensor_copy(ktb[:, qb * 512:(qb + 1) * 512], P[1][:, :]))
                            v.wait_ge(CT.h, v_ps)
                            for sub in range(4):
                                kti = qb * 4 + sub
                                v.tensor_copy(vsb[0][:, kti * 65:kti * 65 + 64],
                                              P[2][:, sub * 128:sub * 128 + 64])
                                inst = v.tensor_copy(vsb[1][:, kti * 65:kti * 65 + 64],
                                                     P[2][:, sub * 128 + 64:sub * 128 + 128])
                            CV.inc(inst)
                        pfree[0] = [(CV, CV.n - 2)]
                        pfree[1] = [(CV, CV.n - 1)]
                        pfree[2] = [(CV, CV.n)]
                        xs_free[xpar] = [(CT, v_ps)]
                    qkv_evacs = CV.n
                    wq_free[wpar] = [(CT, last_vps)]

                    for qb in range(NQB):
                        nkt = 4 * (qb + 1) if causal else NT
                        c0, c1 = (4, 5) if qb % 2 == 0 else (6, 7)
                        mk = set(kti for kti in range(nkt)
                                 if causal and 0 <= kti - 4 * qb <= 3)
                        nmask_upto = {}
                        cnt = 0
                        for kti in range(nkt):
                            if kti in mk:
                                cnt += 1
                            nmask_upto[kti] = cnt
                        ct0 = CT.n
                        cs0 = CS.n
                        cv0m = CV.n
                        s0_mark = lambda i: ct0 + 3 * i + 1
                        s1_mark = lambda i: ct0 + 3 * i + 2
                        pv_mark = lambda i: ct0 + 3 * i + 3
                        exp0_mark = lambda i: cs0 + 2 * i + 1
                        exp1_mark = lambda i: cs0 + 2 * i + 2
                        sfree = [list(pfree[2]), list(pfree[3])]
                        cfree = [list(pfree[c0]), list(pfree[c1])]
                        ptf = [list(x) for x in pt_free]

                        @blk.tensor
                        def _(t):
                            t.wait_ge(CV.h, qkv_evacs)
                            for kti in range(nkt):
                                ppar = kti % 2
                                if kti == 0:
                                    waits(t, sfree[0])
                                else:
                                    t.wait_ge(CS.h, exp0_mark(kti - 1))
                                t.matmul(P[2][:, :], ktb[0:64, kti * 128:(kti + 1) * 128],
                                         qt[0:64, qb * 512:(qb + 1) * 512],
                                         start=True, stop=True).then_inc(CT.h, 1)
                                if kti == 0:
                                    waits(t, sfree[1])
                                else:
                                    t.wait_ge(CS.h, exp1_mark(kti - 1))
                                t.matmul(P[3][:, :], ktb[64:128, kti * 128:(kti + 1) * 128],
                                         qt[64:128, qb * 512:(qb + 1) * 512],
                                         start=True, stop=True,
                                         tile_position=(64, 0)).then_inc(CT.h, 1)
                                if kti == 0:
                                    waits(t, cfree[0])
                                    waits(t, cfree[1])
                                if kti in mk:
                                    t.wait_ge(CV.h, cv0m + nmask_upto[kti])
                                else:
                                    t.wait_ge(CS.h, exp1_mark(kti))
                                t.matmul(P[c0][0:65, :], vsb[0][:, kti * 65:(kti + 1) * 65],
                                         pt[ppar][:, :], start=(kti == 0), stop=(kti == nkt - 1))
                                t.matmul(P[c1][0:65, :], vsb[1][:, kti * 65:(kti + 1) * 65],
                                         pt[2 + ppar][:, :], start=(kti == 0),
                                         stop=(kti == nkt - 1)).then_inc(CT.h, 1)
                        CT.n += 3 * nkt
                        assert CT.n == ct0 + 3 * nkt

                        @blk.scalar
                        def _(sq):
                            for kti in range(nkt):
                                ppar = kti % 2
                                sq.wait_ge(CT.h, s0_mark(kti))
                                if kti >= 2:
                                    sq.wait_ge(CT.h, pv_mark(kti - 2))
                                else:
                                    waits(sq, ptf[ppar])
                                    waits(sq, ptf[2 + ppar])
                                sq.activation(pt[ppar][:, :], P[2][:, :], AF.Exp,
                                              scale=scale).then_inc(CS.h, 1)
                                sq.wait_ge(CT.h, s1_mark(kti))
                                sq.activation(pt[2 + ppar][:, :], P[3][:, :], AF.Exp,
                                              scale=scale).then_inc(CS.h, 1)
                        CS.n += 2 * nkt
                        assert CS.n == cs0 + 2 * nkt

                        if mk:
                            @blk.vector
                            def _(v):
                                for kti in sorted(mk):
                                    ppar = kti % 2
                                    r = kti - 4 * qb
                                    v.wait_ge(CS.h, exp1_mark(kti))
                                    v.tensor_tensor(pt[ppar][:, :], pt[ppar][:, :],
                                                    cm[:, r * 512:(r + 1) * 512], OP.mult)
                                    v.tensor_tensor(pt[2 + ppar][:, :], pt[2 + ppar][:, :],
                                                    cm[:, r * 512:(r + 1) * 512],
                                                    OP.mult).then_inc(CV.h, 1)
                            CV.n += len(mk)
                        assert CV.n == cv0m + len(mk)

                        lp = (nkt - 1) % 2
                        pt_free[lp] = [(CT, pv_mark(nkt - 1))]
                        pt_free[2 + lp] = [(CT, pv_mark(nkt - 1))]
                        pt_free[1 - lp] = [(CT, pv_mark(nkt - 2))]
                        pt_free[3 - lp] = [(CT, pv_mark(nkt - 2))]
                        pfree[2] = [(CS, exp0_mark(nkt - 1))]
                        pfree[3] = [(CS, exp1_mark(nkt - 1))]

                        this_evac = (qb, c0, c1, pv_mark(nkt - 1))
                        if pending_evac is not None:
                            cc_all = emit_evac(*pending_evac)
                        pending_evac = this_evac
                    if pending_evac is not None:
                        cc_all = emit_evac(*pending_evac)
                        pending_evac = None

                for tt in range(NT):
                    blo = 0 if tt % 2 == 0 else 2
                    bhi = blo + 1
                    ctl = [0, 0]
                    for dh in range(2):
                        bk_ = blo if dh == 0 else bhi
                        @blk.tensor
                        def _(t):
                            t.wait_ge(CG.h, g_wo)
                            t.wait_ge(CV.h, cc_all)
                            waits(t, pfree[bk_])
                            for c in range(8):
                                inst = t.matmul(P[bk_][:, :],
                                                cc[:, c * S + tt * 128:c * S + (tt + 1) * 128],
                                                wos[:, c * 1024 + dh * 512:c * 1024 + (dh + 1) * 512],
                                                start=(c == 0), stop=(not causal and c == 7))
                            if causal:
                                inst = t.matmul(P[bk_][:, :], ones[0:1, 0:128],
                                                bor[0:1, dh * 512:(dh + 1) * 512],
                                                start=False, stop=True)
                            CT.inc(inst)
                        ctl[dh] = CT.n

                    boundary_tile(k, tt, "psum", (blo, bhi, ctl[0], ctl[1]), is_last)
                if not is_last:
                    xln_ready = CX.n

        def emit_ffn(k, is_last):
            nonlocal xln_ready
            sub_start_cx = xln_ready
            with ExitStack() as fes:
                A = fes.enter_context
                xt = [A(nc.sbuf_tensor(f"s{k}_xt{i}", [128, 8 * TB], BF16)) for i in range(2)]
                u = A(nc.sbuf_tensor(f"s{k}_u", [128, 32 * TB], BF16))
                acc = A(nc.sbuf_tensor(f"s{k}_acc", [128, 8 * 1024], F32))
                w1b = [A(nc.sbuf_tensor(f"s{k}_w1b{i}", [128, 2048], BF16)) for i in range(2)]
                w2b = [A(nc.sbuf_tensor(f"s{k}_w2b{i}", [128, 512], BF16)) for i in range(2)]
                me = [A(nc.sbuf_tensor(f"s{k}_me{i}", [128, TB], BF16)) for i in range(2)]
                sg = [A(nc.sbuf_tensor(f"s{k}_sg{i}", [128, 512], BF16)) for i in range(2)]
                tsc = A(nc.sbuf_tensor(f"s{k}_tsc", [128, 512], F32))

                xt_free = [[], []]
                w1_free = [[(CX, sub_start_cx)], [(CX, sub_start_cx)]]
                w2_free = [[(CX, sub_start_cx)], [(CX, sub_start_cx)]]
                me_free = [[], []]
                me_ready = [0, 0]
                prev_y_done = 0
                idxb_free = []
                mrow_prev = 0

                for tb in range(NTB):
                    tpar = tb % 2

                    @blk.sync
                    def _(sq):
                        waits(sq, xt_free[tpar])
                        sq.wait_ge(CX.h, xln_ready)
                        for c in range(8):
                            CD.dma(sq.dma_start(
                                out=xt[tpar][:, c * TB:(c + 1) * TB],
                                in_=xln_d[tb * TB:(tb + 1) * TB, c * 128:(c + 1) * 128],
                                transpose=True))
                    xt_ld = CD.n

                    for tt8 in range(TB // 128):
                        tok0 = tb * TB + tt8 * 128

                        @blk.tensor
                        def _(t):
                            t.wait_ge(CD.h, xt_ld)
                            t.wait_ge(CG.h, g_consts)
                            waits(t, pfree[4])
                            for c in range(8):
                                t.matmul(P[4][:, 0:8],
                                         xt[tpar][:, c * TB + tt8 * 128:c * TB + (tt8 + 1) * 128],
                                         rws[:, c * 8:(c + 1) * 8],
                                         start=(c == 0), stop=False)
                            CT.inc(t.matmul(P[4][:, 0:8], ones[0:1, 0:128], rbs[0:1, :],
                                            start=False, stop=True))
                        rt_ps = CT.n

                        @blk.vector
                        def _(v):
                            v.wait_ge(CT.h, rt_ps)
                            CV.inc(v.reduce_max(r8[:, 0:1], P[4][:, 0:8], axis=AX))
                            v.wait_ge(CV.h, CV.n)
                            CV.inc(v.tensor_copy(r8b[:, :], bass.AP(r8, 0, [[9, 128], [0, 8]])))
                            v.wait_ge(CV.h, CV.n)
                            CV.inc(v.max_index(idx8[:, :], r8b[:, :], P[4][:, 0:8]))
                            v.wait_ge(CV.h, CV.n)
                            waits(v, idxb_free)
                            CV.inc(v.tensor_copy(idxb[:, :], idx8[:, 0:1]))
                        idx_done = CV.n
                        pfree[4] = [(CV, idx_done)]

                        @blk.sync
                        def _(sy):
                            sy.wait_ge(CV.h, idx_done)
                            CX.dma(sy.dma_start(
                                out=idx_d[0:1, tok0:tok0 + 128].rearrange("one t -> t one"),
                                in_=idxb[:, :]))
                        idxb_free = [(CX, CX.n)]
                    idx_stores = CX.n

                    @blk.sync
                    def _(sy):
                        sy.wait_ge(CX.h, idx_stores)
                        sy.wait_ge(CV.h, mrow_prev)
                        CX.dma(sy.dma_start(out=idxT[0:1, :],
                                            in_=idx_d[0:1, tb * TB:(tb + 1) * TB]))
                    idxT_ld = CX.n

                    def emit_me(e):
                        nonlocal mrow_prev
                        mpar = e % 2

                        @blk.vector
                        def _(v):
                            v.wait_ge(CX.h, idxT_ld)
                            v.wait_ge(CT.h, prev_y_done)
                            CV.inc(v.tensor_single_scalar(mrow[mpar][0:1, :], idxT[0:1, :],
                                                          float(e), OP.is_equal))
                        mrow_done = CV.n
                        mrow_prev = mrow_done

                        @blk.tensor
                        def _(t):
                            t.wait_ge(CV.h, mrow_done)
                            waits(t, pfree[4])
                            CT.inc(t.matmul(P[4][:, :], ones[0:1, 0:128],
                                            mrow[mpar][0:1, 0:512], start=True, stop=True))
                            waits(t, pfree[5])
                            CT.inc(t.matmul(P[5][:, :], ones[0:1, 0:128],
                                            mrow[mpar][0:1, 512:TB],
                                            start=True, stop=True))
                        ps4 = CT.n - 1
                        ps5 = CT.n

                        @blk.vector
                        def _(v):
                            waits(v, me_free[mpar])
                            v.wait_ge(CT.h, ps4)
                            CV.inc(v.tensor_copy(me[mpar][:, 0:512], P[4][:, :]))
                            v.wait_ge(CT.h, ps5)
                            CV.inc(v.tensor_copy(me[mpar][:, 512:1024], P[5][:, :]))
                        pfree[4] = [(CV, CV.n - 1)]
                        pfree[5] = [(CV, CV.n)]
                        me_ready[mpar] = CV.n

                    emit_me(0)

                    for e in range(E):
                        mpar = e % 2
                        ct0h = CT.n
                        cs0h = CS.n
                        cv0h = CV.n
                        a_mark = lambda i: ct0h + 2 * i + 1
                        g_mark = lambda i: ct0h + 2 * i + 2
                        sg_mark = lambda i: cs0h + i + 1
                        stt_mark = lambda i: cv0h + 2 * i + 1
                        u_mark = lambda i: cv0h + 2 * i + 2
                        w1_marks = {}

                        @blk.gpsimd
                        def _(g):
                            for j in range(32):
                                if j < 2:
                                    waits(g, w1_free[j % 2])
                                else:
                                    g.wait_ge(CT.h, g_mark(2 * (j - 2) + 1))
                                CG.dma(g.dma_start(out=w1b[j % 2][:, :], in_=w1h[e, j]))
                                w1_marks[j] = CG.n

                        pf_snap = [list(pfree[b]) for b in range(4)]

                        @blk.tensor
                        def _(t):
                            for i in range(64):
                                j, sb = divmod(i, 2)
                                wpar = j % 2
                                ba = 2 * sb
                                bg = ba + 1
                                if sb == 0:
                                    t.wait_ge(CG.h, w1_marks[j])
                                if i < 2:
                                    waits(t, pf_snap[ba])
                                    waits(t, pf_snap[bg])
                                else:
                                    t.wait_ge(CV.h, stt_mark(i - 2))
                                    t.wait_ge(CS.h, sg_mark(i - 2))
                                for c in range(8):
                                    inst = t.matmul(
                                        P[ba][:, :], w1b[wpar][:, c * 128:(c + 1) * 128],
                                        xt[tpar][:, c * TB + sb * 512:c * TB + (sb + 1) * 512],
                                        start=(c == 0), stop=(c == 7))
                                inst.then_inc(CT.h, 1)
                                for c in range(8):
                                    inst = t.matmul(
                                        P[bg][:, :], w1b[wpar][:, 1024 + c * 128:1024 + (c + 1) * 128],
                                        xt[tpar][:, c * TB + sb * 512:c * TB + (sb + 1) * 512],
                                        start=(c == 0), stop=(c == 7))
                                inst.then_inc(CT.h, 1)
                        CT.n += 128
                        assert CT.n == ct0h + 128

                        sg_snap = [list(sg_free[0]), list(sg_free[1])]

                        @blk.scalar
                        def _(sq):
                            for i in range(64):
                                j, sb = divmod(i, 2)
                                bg = 2 * sb + 1
                                spar = i % 2
                                gcol = e * 64 + 32 + j
                                sq.wait_ge(CT.h, g_mark(i))
                                if i < 2:
                                    waits(sq, sg_snap[spar])
                                else:
                                    sq.wait_ge(CV.h, stt_mark(i - 2))
                                sq.activation(sg[spar][:, :], P[bg][:, :], AF.Sigmoid,
                                              bias=b1s[:, gcol:gcol + 1]).then_inc(CS.h, 1)
                        CS.n += 64
                        assert CS.n == cs0h + 64

                        @blk.vector
                        def _(v):
                            for i in range(64):
                                j, sb = divmod(i, 2)
                                ba = 2 * sb
                                spar = i % 2
                                bcol = e * 64 + j
                                v.wait_ge(CT.h, a_mark(i))
                                v.wait_ge(CS.h, sg_mark(i))
                                if i == 0:
                                    v.wait_ge(CV.h, me_ready[mpar])
                                    v.wait_ge(CT.h, prev_y_done)
                                v.scalar_tensor_tensor(tsc[:, :], P[ba][:, :],
                                                       b1s[:, bcol:bcol + 1],
                                                       sg[spar][:, :], OP.add,
                                                       OP.mult).then_inc(CV.h, 1)
                                v.tensor_tensor(
                                    u[:, j * TB + sb * 512:j * TB + (sb + 1) * 512],
                                    tsc[:, :], me[mpar][:, sb * 512:(sb + 1) * 512],
                                    OP.mult).then_inc(CV.h, 1)
                        CV.n += 128
                        assert CV.n == cv0h + 128

                        u_done = CV.n
                        pfree[0] = [(CV, stt_mark(62))]
                        pfree[1] = [(CS, sg_mark(62))]
                        pfree[2] = [(CV, stt_mark(63))]
                        pfree[3] = [(CS, sg_mark(63))]
                        sg_free[0] = [(CV, stt_mark(62))]
                        sg_free[1] = [(CV, stt_mark(63))]
                        w1_free[0] = [(CT, g_mark(61))]
                        w1_free[1] = [(CT, g_mark(63))]
                        me_free[mpar] = [(CV, u_done)]

                        if e + 1 < E:
                            emit_me(e + 1)

                        w2_marks = {}
                        w2_cons = {}

                        def w2_load(seq):
                            dh_, j_ = divmod(seq, 32)
                            wpar2 = seq % 2
                            @blk.gpsimd
                            def _(g):
                                if seq >= 2:
                                    g.wait_ge(CT.h, w2_cons[seq - 2])
                                else:
                                    waits(g, w2_free[wpar2])
                                CG.dma(g.dma_start(out=w2b[wpar2][:, :], in_=w2h[e, dh_, j_]))
                            w2_marks[seq] = CG.n

                        w2_load(0)
                        w2_load(1)

                        y_ts_marks = [0] * 8
                        for dh in range(2):
                            @blk.tensor
                            def _(t):
                                for ts in range(8):
                                    waits(t, pfree[ts])
                                    inst = t.matmul(
                                        P[ts][:, :],
                                        mrow[e % 2][0:1, ts * 128:(ts + 1) * 128],
                                        b2r[0:1, e * 1024 + dh * 512:e * 1024 + (dh + 1) * 512],
                                        start=True, stop=False)
                                CT.inc(inst)

                            for j in range(32):
                                seq = dh * 32 + j
                                wpar2 = seq % 2

                                @blk.tensor
                                def _(t):
                                    t.wait_ge(CG.h, w2_marks[seq])
                                    t.wait_ge(CV.h, u_done)
                                    for ts in range(8):
                                        inst = t.matmul(
                                            P[ts][:, :],
                                            u[:, j * TB + ts * 128:j * TB + (ts + 1) * 128],
                                            w2b[wpar2][:, :], start=False, stop=(j == 31))
                                        if j == 31:
                                            CT.inc(inst)
                                    if j != 31:
                                        CT.inc(inst)
                                w2_cons[seq] = CT.n
                                if seq + 2 < 64:
                                    w2_load(seq + 2)
                                if j == 31:
                                    for ts in range(8):
                                        y_ts_marks[ts] = CT.n - 7 + ts

                            @blk.vector
                            def _(v):
                                for ts in range(8):
                                    v.wait_ge(CT.h, y_ts_marks[ts])
                                    dst = acc[:, ts * 1024 + dh * 512:ts * 1024 + (dh + 1) * 512]
                                    if e == 0:
                                        CV.inc(v.tensor_copy(dst, P[ts][:, :]))
                                    else:
                                        CV.inc(v.tensor_tensor(dst, P[ts][:, :], dst, OP.add))
                            for ts in range(8):
                                pfree[ts] = [(CV, CV.n - 7 + ts)]
                            if e == E - 1 and dh == 1:
                                acc_ready = [CV.n - 7 + ts for ts in range(8)]
                        w2_free[0] = [(CT, w2_cons[62])]
                        w2_free[1] = [(CT, w2_cons[63])]
                        prev_y_done = CT.n

                    for ts in range(8):
                        tt = tb * 8 + ts
                        boundary_tile(k, tt, "sbuf", (acc, ts * 1024, acc_ready[ts]), is_last)
                    xt_free[tpar] = [(CT, prev_y_done)]
                if not is_last:
                    xln_ready = CX.n

        sg_free = [[], []]

        for k in range(n_sub):
            kind = SUBLAYERS[k]
            is_last = (k == n_sub - 1)
            if kind == "ffn":
                emit_ffn(k, is_last)
            else:
                emit_attn(k, "a" if kind == "attn_a" else "b", is_last)

        final_cx = CX.n

        @blk.sync
        def _(sy):
            sy.wait_ge(CX.h, final_cx)

    return nc



def _prep_weights(ins):
    f32 = np.float32

    def c(x):
        return np.ascontiguousarray(np.asarray(x, dtype=f32))

    wq, wk, wv = ins["L0_attn_a_Wq"], ins["L0_attn_a_Wk"], ins["L0_attn_a_Wv"]
    qkv = np.stack([np.asarray(wq), np.asarray(wk), np.asarray(wv)], axis=1)
    qkv = qkv.reshape(8, 2, 3, 8, 128, 64)
    qkv = qkv.transpose(0, 4, 2, 3, 1, 5)
    wqkv_a = c(qkv.reshape(8, 128, 3072))

    kb = np.asarray(ins["L0_attn_b_kernel"])
    qkvb = np.transpose(kb, (1, 0, 2, 3)).reshape(8, 2, 3, 8, 128, 64)
    qkvb = qkvb.transpose(0, 4, 2, 3, 1, 5)
    wqkv_b = c(qkvb.reshape(8, 128, 3072))

    wo = np.asarray(ins["L0_attn_a_Wo"])
    wo_a = c(wo.reshape(8, 128, 1024).transpose(1, 0, 2).reshape(128, 8192))
    wob = np.asarray(ins["L0_attn_b_WO"])[0]
    wob_r = wob.reshape(64, 16, 1024).transpose(1, 0, 2).reshape(1024, 1024)
    wo_b = c(wob_r.reshape(8, 128, 1024).transpose(1, 0, 2).reshape(128, 8192))

    bq = np.asarray(ins["L0_attn_a_bq"]).reshape(8, 2, 64).transpose(1, 2, 0).reshape(128, 8)
    bk = np.asarray(ins["L0_attn_a_bk"]).reshape(8, 2, 64).transpose(1, 2, 0).reshape(128, 8)
    bqk_a = c(np.concatenate([bq, bk], axis=1))
    bv_a = c(np.asarray(ins["L0_attn_a_bv"]).reshape(1, 1024))
    bo_a = c(np.asarray(ins["L0_attn_a_bo"]).reshape(1, 1024))

    w1 = np.asarray(ins["L0_ffn_w1"])
    w1h = c(w1.reshape(8, 8, 128, 2, 32, 128).transpose(0, 4, 2, 3, 1, 5)
            .reshape(E, 32, 128, 2048))
    w2 = np.asarray(ins["L0_ffn_w2"])
    w2h = c(w2.reshape(8, 32, 128, 2, 512).transpose(0, 3, 1, 2, 4))
    b1 = np.asarray(ins["L0_ffn_b1"])
    b1h = c(b1.reshape(8, 2, 32, 128).transpose(3, 0, 1, 2).reshape(128, E * 64))
    b2h = c(np.asarray(ins["L0_ffn_b2"]).reshape(1, E * 1024))
    rwh = c(np.asarray(ins["L0_ffn_router_w"]).reshape(8, 128, 8).transpose(1, 0, 2)
            .reshape(128, 64))
    rbh = c(np.asarray(ins["L0_ffn_router_b"]).reshape(1, 8))

    pp = np.arange(128)[:, None]
    ff = np.arange(512)[None, :]
    cmask = c(np.concatenate([((pp + 128 * r) <= ff).astype(f32) for r in range(4)], axis=1))
    idn_h = c(np.eye(128, dtype=f32))

    return dict(wqkv_a=wqkv_a, wqkv_b=wqkv_b, wo_a=wo_a, wo_b=wo_b, bqk_a=bqk_a,
                bv_a=bv_a, bo_a=bo_a, w1h=w1h, w2h=w2h, b1h=b1h, b2h=b2h,
                rwh=rwh, rbh=rbh, cmask=cmask, idn_h=idn_h)


def kernel(**inputs):
    emb = np.ascontiguousarray(np.asarray(inputs["L0_attn_a_embeddings"], dtype=np.float32))
    w = _prep_weights(inputs)
    nc = build_nc(4)
    in_maps = []
    for core in range(8):
        m = dict(w)
        m["x0"] = np.ascontiguousarray(emb[core])
        in_maps.append(m)
    res = run_bass_kernel_spmd(nc, in_maps, core_ids=list(range(8)))
    outp = np.stack([res.results[c]["out"] for c in range(8)], axis=0)
    return outp.astype(np.float32)
